# version 05 (final)
# baseline (speedup 1.0000x reference)
import contextlib
import math
import numpy as np
import ml_dtypes
import concourse.bass as bass
import concourse.mybir as mybir
from concourse.bass_utils import run_bass_kernel_spmd

F32 = mybir.dt.float32
BF16 = mybir.dt.bfloat16
F16 = mybir.dt.float16
I32 = mybir.dt.int32
AF = mybir.ActivationFunctionType
ALU = mybir.AluOpType
AX = mybir.AxisListType

S = 2048
D = 4096
NT = S // 128
NCH = D // 128
NTB = S // 512

SC0 = 1.0 / 32.0
B0 = 12.0
SC1 = 1.0 / math.sqrt(128.0)
B1 = 16.0


def build(dbg=0):
    nc = bass.Bass("TRN2", target_bir_lowering=False)
    scratch_kind = "ExternalOutput" if dbg else "Internal"
    kfm_o = None

    x_d = nc.dram_tensor("x", [S, D], F32, kind="ExternalInput").ap()
    wqkv_d = nc.dram_tensor("wqkv", [D, 3072], F32, kind="ExternalInput").ap()
    wo_d = nc.dram_tensor("wo", [1024, D], F32, kind="ExternalInput").ap()
    wbase_d = nc.dram_tensor("wbase", [D, D], F32, kind="ExternalInput").ap()
    wrout_d = nc.dram_tensor("wrout", [8, D], F32, kind="ExternalInput").ap()
    amat_d = nc.dram_tensor("amat", [128, D], F32, kind="ExternalInput").ap()
    bmat_d = nc.dram_tensor("bmat", [8, D, 16], F32, kind="ExternalInput").ap()
    wq_d = nc.dram_tensor("wq", [D, D], F32, kind="ExternalInput").ap()
    wk_d = nc.dram_tensor("wk", [D, 1024], F32, kind="ExternalInput").ap()
    wv_d = nc.dram_tensor("wv", [D, 1024], F32, kind="ExternalInput").ap()
    pk_d = nc.dram_tensor("pk", [D, 2048], I32, kind="ExternalInput").ap()
    sc_d = nc.dram_tensor("sc", [D, 128], F16, kind="ExternalInput").ap()
    biases_d = {n: nc.dram_tensor(n, [sz], F32, kind="ExternalInput").ap()
                for n, sz in (("bqkv", 3072), ("bo", D), ("bbase", D),
                              ("bq", D), ("bk", 1024), ("bv", 1024))}
    ident_d = nc.dram_tensor("identc", [128, 128], BF16, kind="ExternalInput").ap()
    onesc_d = nc.dram_tensor("onesc", [128, 512], BF16, kind="ExternalInput").ap()
    cos_d = nc.dram_tensor("cosc", [128, S], BF16, kind="ExternalInput").ap()
    sin_d = nc.dram_tensor("sinc", [128, S], BF16, kind="ExternalInput").ap()
    rep_d = nc.dram_tensor("repc", [8, 128], BF16, kind="ExternalInput").ap()

    y_d = nc.dram_tensor("y", [S, D], F32, kind="ExternalOutput").ap()

    xn_s = nc.dram_tensor("xn_s", [NCH, 128, S], BF16, kind=scratch_kind).ap()
    q_s = nc.dram_tensor("q_s", [NCH, 128, S], BF16, kind=scratch_kind).ap()
    r1_s = nc.dram_tensor("r1_s", [S, D], F32, kind=scratch_kind).ap()
    ax_s = nc.dram_tensor("ax_s", [S, D], F32, kind=scratch_kind).ap()
    f_s = nc.dram_tensor("f_s", [NCH, 128, S], BF16, kind=scratch_kind).ap()
    if dbg:
        kfm_o = nc.dram_tensor("kfm_o", [128, 8 * S], BF16, kind="ExternalOutput").ap()
        vaug_o = nc.dram_tensor("vaug_o", [128, NT * 8 * 132], BF16, kind="ExternalOutput").ap()

    prog = {k: [] for k in ("sync", "gpsimd", "tensor", "vector", "scalar")}
    es = contextlib.ExitStack()
    sem_names = ("LD", "ST", "TE", "VE", "SE", "F2", "LX")
    sems = {n: es.enter_context(nc.semaphore(n)) for n in sem_names}
    cnt = {n: 0 for n in sem_names}

    def inc(name, k=1):
        cnt[name] += k
        h = sems[name]
        return lambda ins: ins.then_inc(h, k)

    def wait(eng, name, val=None):
        v = cnt[name] if val is None else val
        if v <= 0:
            return
        h = sems[name]
        prog[eng].append(lambda e, h=h, v=v: e.wait_ge(h, v))

    def snap():
        return dict(cnt)

    def wait_snap(eng, sn, names=None):
        for n in (names or sem_names):
            wait(eng, n, sn[n])

    def barrier():
        sn = snap()
        for eng in prog:
            wait_snap(eng, sn)

    def load(dst, src, s="LD"):
        f = inc(s, 16)
        prog["sync"].append(lambda e, dst=dst, src=src, f=f: f(e.dma_start(out=dst, in_=src)))

    def store(dst, src):
        f = inc("ST", 16)
        prog["gpsimd"].append(lambda e, dst=dst, src=src, f=f: f(e.dma_start(out=dst, in_=src)))

    def vop(method, *a, s=None, **kw):
        f = inc(s) if s else None
        def run(e, method=method, a=a, kw=kw, f=f):
            ins = getattr(e, method)(*a, **kw)
            if f:
                f(ins)
        prog["vector"].append(run)

    def vch(method, *a, **kw):
        vop(method, *a, s="VE", **kw)
        wait("vector", "VE")

    def sop(method, *a, s=None, **kw):
        f = inc(s) if s else None
        def run(e, method=method, a=a, kw=kw, f=f):
            ins = getattr(e, method)(*a, **kw)
            if f:
                f(ins)
        prog["scalar"].append(run)

    def mm(out, lhsT, rhs, start, stop, s=None):
        f = inc(s) if s else None
        def run(e, out=out, lhsT=lhsT, rhs=rhs, start=start, stop=stop, f=f):
            ins = e.matmul(out, lhsT, rhs, start=start, stop=stop)
            if f:
                f(ins)
        prog["tensor"].append(run)

    ident = es.enter_context(nc.sbuf_tensor("ident", [128, 128], BF16))
    ones = es.enter_context(nc.sbuf_tensor("ones", [128, 512], BF16))
    rep = es.enter_context(nc.sbuf_tensor("rep", [8, 128], BF16))
    rp_a = es.enter_context(nc.sbuf_tensor("rp_a", [64, 512], BF16))
    rp_b = es.enter_context(nc.sbuf_tensor("rp_b", [64, 512], BF16))
    qstg = es.enter_context(nc.sbuf_tensor("qstg", [128, 512], BF16))
    qrop = es.enter_context(nc.sbuf_tensor("qrop", [128, 512], BF16))
    hw = es.enter_context(nc.sbuf_tensor("hw", [128, S], BF16))
    ps = es.enter_context(nc.psum_tensor("ps", [128, 4096], F32))

    def bank(i):
        return ps[:, i * 512:(i + 1) * 512]

    cst_vals = (-B0, -B1, SC0, SC1)
    for i, v in enumerate(cst_vals):
        t = es.enter_context(nc.sbuf_tensor(f"cst{i}", [128, 1], F32))
        nc.const_aps.register(F32, float(v), t[:, :])
        vop("memset", t[:, :], float(v), s="VE")
    load(ident[:, :], ident_d[:, :])
    load(ones[:, :], onesc_d[:, :])
    load(rep[:, :], rep_d[:, :])
    barrier()

    bias_sizes = dict(bqkv=3072, bo=D, bbase=D, bq=D, bk=1024, bv=1024)

    def bias_setup(ph, tag, names):
        out = {}
        stg = ph.enter_context(nc.sbuf_tensor(f"bst_{tag}", [1, 1024], F32))
        for n in names:
            sz = bias_sizes[n]
            out[n] = ph.enter_context(nc.sbuf_tensor(f"bb_{tag}_{n}", [1, sz], BF16))
            for c0 in range(0, sz, 1024):
                w = min(1024, sz - c0)
                wait("sync", "VE")
                load(stg[:, 0:w], biases_d[n][None, c0:c0 + w])
                wait("vector", "LD")
                vop("tensor_copy", out=out[n][:, c0:c0 + w], in_=stg[:, 0:w], s="VE")
        return out

    def ph_ln(l, resid):
        with contextlib.ExitStack() as ph:
            x_sb = ph.enter_context(nc.sbuf_tensor(f"lnx{l}", [128, D], F32))
            xsq = ph.enter_context(nc.sbuf_tensor(f"lnq{l}", [128, D], F32))
            xnb = ph.enter_context(nc.sbuf_tensor(f"lnn{l}", [128, D], BF16))
            stg = ph.enter_context(nc.sbuf_tensor(f"lns{l}", [128, 512], BF16))
            sm = ph.enter_context(nc.sbuf_tensor(f"lnm{l}", [128, 1], F32))
            mu = ph.enter_context(nc.sbuf_tensor(f"lnu{l}", [128, 1], F32))
            sq = ph.enter_context(nc.sbuf_tensor(f"ln2{l}", [128, 1], F32))
            var = ph.enter_context(nc.sbuf_tensor(f"lnv{l}", [128, 1], F32))
            rst = ph.enter_context(nc.sbuf_tensor(f"lnr{l}", [128, 1], F32))
            for tt in range(NT):
                wait("sync", "VE")
                wait("sync", "SE")
                load(x_sb[:, :], resid[tt * 128:(tt + 1) * 128, :])
                wait("scalar", "LD")
                sop("square", out=xsq[:, :], in_=x_sb[:, :], s="SE")
                wait("vector", "LD")
                vch("reduce_sum", out=sm[:, :], in_=x_sb[:, :], axis=AX.X)
                vch("tensor_scalar_mul", out=mu[:, :], in0=sm[:, :], scalar1=1.0 / D)
                wait("vector", "SE")
                vch("reduce_sum", out=sq[:, :], in_=xsq[:, :], axis=AX.X)
                vch("tensor_scalar_mul", out=var[:, :], in0=sq[:, :], scalar1=1.0 / D)
                vch("tensor_tensor", out=sq[:, :], in0=mu[:, :], in1=mu[:, :], op=ALU.mult)
                vch("tensor_sub", out=var[:, :], in0=var[:, :], in1=sq[:, :])
                vch("tensor_scalar_add", out=var[:, :], in0=var[:, :], scalar1=1e-5)
                vop("reciprocal", out=var[:, :], in_=var[:, :], s="VE")
                wait("scalar", "VE")
                sop("activation", out=rst[:, :], in_=var[:, :], func=AF.Sqrt, s="SE")
                wait("vector", "SE")
                vop("tensor_scalar", out=xnb[:, :], in0=x_sb[:, :], scalar1=mu[:, 0:1],
                    scalar2=rst[:, 0:1], op0=ALU.subtract, op1=ALU.mult, s="VE")
                wait("tensor", "VE")
                for g in range(8):
                    wait("tensor", "VE")
                    for j in range(4):
                        c = g * 4 + j
                        mm(ps[:, (g % 2) * 512 + j * 128:(g % 2) * 512 + (j + 1) * 128],
                           xnb[:, c * 128:(c + 1) * 128], ident[:, :], True, True,
                           s="TE" if j == 3 else None)
                    wait("vector", "TE")
                    wait("vector", "ST")
                    vop("tensor_copy", out=stg[:, :], in_=bank(g % 2), s="VE")
                    wait("gpsimd", "VE")
                    for j in range(4):
                        c = g * 4 + j
                        store(xn_s[c, :, tt * 128:(tt + 1) * 128], stg[:, j * 128:(j + 1) * 128])
        barrier()

    def emit_proj(tag, xnh, th, wsrc, col0, ocs_all, bias_ap, bias_off, sink, gsz=4):
        with contextlib.ExitStack() as ph:
            wtf = [ph.enter_context(nc.sbuf_tensor(f"{tag}f{i}", [128, 512], F32)) for i in range(2)]
            wtb = [ph.enter_context(nc.sbuf_tensor(f"{tag}b{i}", [128, 512], BF16)) for i in range(2)]
            guard_ve = [cnt["VE"], cnt["VE"]]
            guard_te = [cnt["TE"], cnt["TE"]]
            for og in range(0, len(ocs_all), gsz):
                ocs = ocs_all[og:og + gsz]
                sn = snap()
                wait_snap("tensor", sn, ("VE", "SE", "ST"))
                for inc_i in range(NCH):
                    buf = inc_i % 2
                    wait("sync", "VE", guard_ve[buf])
                    load(wtf[buf][:, 0:128 * len(ocs)],
                         wsrc[inc_i * 128:(inc_i + 1) * 128,
                              col0 + ocs[0] * 128: col0 + ocs[0] * 128 + 128 * len(ocs)])
                    wait("vector", "LD")
                    wait("vector", "TE", guard_te[buf])
                    vop("tensor_copy", out=wtb[buf][:, 0:128 * len(ocs)],
                        in_=wtf[buf][:, 0:128 * len(ocs)], s="VE")
                    guard_ve[buf] = cnt["VE"]
                    wait("tensor", "VE")
                    for j in range(len(ocs)):
                        for tb in range(2):
                            mm(bank(j * 2 + tb), wtb[buf][:, j * 128:(j + 1) * 128],
                               xnh[:, inc_i * 1024 + tb * 512: inc_i * 1024 + (tb + 1) * 512],
                               inc_i == 0, False,
                               s="TE" if (j == len(ocs) - 1 and tb == 1) else None)
                    guard_te[buf] = cnt["TE"]
                for j, oc in enumerate(ocs):
                    for tb in range(2):
                        mm(bank(j * 2 + tb),
                           bias_ap[0:1, (bias_off + oc) * 128:(bias_off + oc) * 128 + 128],
                           ones[0:1, 0:512], False, True,
                           s="TE" if (j == len(ocs) - 1 and tb == 1) else None)
                for j, oc in enumerate(ocs):
                    for tb in range(2):
                        sink(oc, 2 * th + tb, bank(j * 2 + tb))

    ROPE = {}

    def rope_emit(dst0, dst1, src, c0, c1):
        n = c1 - c0
        t1 = src[0:64, 0:n]
        t2 = src[64:128, 0:n]
        cs0 = ROPE["cos"][0:64, c0:c1]
        cs1 = ROPE["cos"][64:128, c0:c1]
        sn0 = ROPE["sin"][0:64, c0:c1]
        sn1 = ROPE["sin"][64:128, c0:c1]
        vop("tensor_tensor", out=rp_a[0:64, 0:n], in0=t1, in1=cs0, op=ALU.mult)
        vop("tensor_tensor", out=rp_b[0:64, 0:n], in0=t2, in1=sn1, op=ALU.mult, s="VE")
        wait("vector", "VE")
        vop("tensor_sub", out=dst0, in0=rp_a[0:64, 0:n], in1=rp_b[0:64, 0:n])
        vop("tensor_tensor", out=rp_a[0:64, 0:n], in0=t1, in1=sn0, op=ALU.mult)
        vop("tensor_tensor", out=rp_b[0:64, 0:n], in0=t2, in1=cs1, op=ALU.mult, s="VE")
        wait("vector", "VE")
        vop("tensor_add", out=dst1, in0=rp_a[0:64, 0:n], in1=rp_b[0:64, 0:n], s="VE")

    def emit_router_half(th, xnh, R):
        if th == 0:
            with contextlib.ExitStack() as ph:
                wr8 = ph.enter_context(nc.sbuf_tensor("wr8", [8, 512], F32))
                wr8b = ph.enter_context(nc.sbuf_tensor("wr8b", [8, 512], BF16))
                atf = ph.enter_context(nc.sbuf_tensor("atf", [128, 512], F32))
                atb = ph.enter_context(nc.sbuf_tensor("atb", [128, 512], BF16))
                for ib in range(8):
                    wait("sync", "VE")
                    wait("sync", "TE")
                    load(wr8[:, :], wrout_d[:, ib * 512:(ib + 1) * 512])
                    wait("vector", "LD")
                    wait("vector", "TE")
                    vop("tensor_copy", out=wr8b[:, :], in_=wr8[:, :], s="VE")
                    wait("tensor", "VE")
                    for j in range(4):
                        c = ib * 4 + j
                        wait("tensor", "VE")
                        mm(ps[0:128, 6 * 512:6 * 512 + 8], wr8b[:, j * 128:(j + 1) * 128],
                           ident[0:8, 0:8], True, True, s="TE")
                        wait("vector", "TE")
                        vop("tensor_copy", out=R["wrt"][:, c * 8:(c + 1) * 8],
                            in_=ps[:, 6 * 512:6 * 512 + 8], s="VE")
                for ib in range(8):
                    sn = snap()
                    wait_snap("sync", sn, ("VE",))
                    load(atf[:, :], amat_d[:, ib * 512:(ib + 1) * 512])
                    wait("vector", "LD")
                    wait("vector", "TE")
                    vop("tensor_copy", out=atb[:, :], in_=atf[:, :], s="VE")
                    wait("tensor", "VE")
                    for j in range(4):
                        mm(ps[:, 6 * 512 + j * 128:6 * 512 + (j + 1) * 128],
                           atb[:, j * 128:(j + 1) * 128], ident[:, :], True, True,
                           s="TE" if j == 3 else None)
                    wait("vector", "TE")
                    vop("tensor_copy", out=R["at"][:, ib * 512:(ib + 1) * 512],
                        in_=bank(6), s="VE")
                wait("tensor", "VE")
        sn = snap()
        wait_snap("tensor", sn, ("VE", "SE", "ST"))
        for inc_i in range(NCH):
            for tb in range(2):
                mm(bank(4 + tb), R["at"][:, inc_i * 128:(inc_i + 1) * 128],
                   xnh[:, inc_i * 1024 + tb * 512: inc_i * 1024 + (tb + 1) * 512],
                   inc_i == 0, inc_i == NCH - 1,
                   s="TE" if inc_i == NCH - 1 and tb == 1 else None)
        wait("vector", "TE")
        vop("tensor_copy", out=R["h"][:, th * 1024: th * 1024 + 512], in_=bank(4))
        vop("tensor_copy", out=R["h"][:, th * 1024 + 512: th * 1024 + 1024], in_=bank(5), s="VE")
        wait("tensor", "VE")
        for tt8 in range(8):
            tt = th * 8 + tt8
            wait("tensor", "VE")
            for inc_i in range(NCH):
                mm(ps[:, 6 * 512:6 * 512 + 8],
                   xnh[:, inc_i * 1024 + tt8 * 128: inc_i * 1024 + (tt8 + 1) * 128],
                   R["wrt"][:, inc_i * 8:(inc_i + 1) * 8], inc_i == 0, inc_i == NCH - 1,
                   s="TE" if inc_i == NCH - 1 else None)
            wait("vector", "TE")
            vch("tensor_copy", out=R["lg"][:, :], in_=ps[:, 6 * 512:6 * 512 + 8])
            vch("reduce_max", out=R["m1"][:, :], in_=R["lg"][:, :], axis=AX.X)
            vch("tensor_tensor", out=R["eq"][:, :], in0=R["lg"][:, :],
                in1=R["m1"][:, 0:1].broadcast_to([128, 8]), op=ALU.is_ge)
            vch("tensor_scalar_mul", out=R["eq"][:, :], in0=R["eq"][:, :], scalar1=-32768.0)
            vch("tensor_add", out=R["eq"][:, :], in0=R["eq"][:, :], in1=R["lg"][:, :])
            vch("reduce_max", out=R["m2"][:, :], in_=R["eq"][:, :], axis=AX.X)
            vop("tensor_scalar", out=R["lgm"][:, :], in0=R["lg"][:, :], scalar1=R["m1"][:, 0:1],
                scalar2=None, op0=ALU.subtract)
            vop("tensor_scalar", out=R["m2m"][:, :], in0=R["m2"][:, :], scalar1=R["m1"][:, 0:1],
                scalar2=None, op0=ALU.subtract, s="VE")
            wait("scalar", "VE")
            sop("activation", out=R["ex"][:, :], in_=R["lgm"][:, :], func=AF.Exp)
            sop("activation", out=R["emx"][:, :], in_=R["m2m"][:, :], func=AF.Exp, s="SE")
            wait("vector", "SE")
            vch("tensor_scalar", out=R["den"][:, :], in0=R["emx"][:, :], scalar1=1.0,
                scalar2=0.5, op0=ALU.add, op1=ALU.mult)
            vch("reciprocal", out=R["rd"][:, :], in_=R["den"][:, :])
            vch("tensor_tensor", out=R["wt"][:, :], in0=R["lg"][:, :],
                in1=R["m2"][:, 0:1].broadcast_to([128, 8]), op=ALU.is_ge)
            vch("tensor_mul", out=R["wt"][:, :], in0=R["wt"][:, :], in1=R["ex"][:, :])
            vch("tensor_tensor", out=R["wt"][:, :], in0=R["wt"][:, :],
                in1=R["rd"][:, 0:1].broadcast_to([128, 8]), op=ALU.mult)
            vop("tensor_copy", out=R["wtb"][:, :], in_=R["wt"][:, :], s="VE")
            wait("tensor", "VE")
            mm(ps[0:8, 6 * 512:6 * 512 + 128], R["wtb"][:, :], ident[:, :], True, True, s="TE")
            wait("vector", "TE")
            vop("tensor_copy", out=R["wfm"][0:8, tt * 128:(tt + 1) * 128],
                in_=ps[0:8, 6 * 512:6 * 512 + 128], s="VE")
            wait("tensor", "VE")
        if th == 1:
            for tb in range(NTB):
                mm(bank(6), rep[:, :], R["wfm"][0:8, tb * 512:(tb + 1) * 512], True, True, s="TE")
                wait("vector", "TE")
                vch("tensor_copy", out=R["stgb"][:, :], in_=bank(6))
                vop("tensor_tensor", out=hw[:, tb * 512:(tb + 1) * 512],
                    in0=R["h"][:, tb * 512:(tb + 1) * 512],
                    in1=R["stgb"][:, :], op=ALU.mult, s="VE")
                wait("tensor", "VE")

    def ph_proj(l, kfm, vaug):
        with contextlib.ExitStack() as pp:
            if l == 0:
                R = {}
                R["wrt"] = pp.enter_context(nc.sbuf_tensor("wrt", [128, NCH * 8], BF16))
                R["at"] = pp.enter_context(nc.sbuf_tensor("at", [128, NCH * 128], BF16))
                R["h"] = pp.enter_context(nc.sbuf_tensor("hsb", [128, S], BF16))
                R["wfm"] = pp.enter_context(nc.sbuf_tensor("wfm", [8, S], BF16))
                R["stgb"] = pp.enter_context(nc.sbuf_tensor("rstgb", [128, 512], BF16))
                for n, w in (("lg", 8), ("m1", 1), ("m2", 1), ("eq", 8), ("lgm", 8),
                             ("m2m", 1), ("ex", 8), ("emx", 1), ("den", 1), ("rd", 1), ("wt", 8)):
                    R[n] = pp.enter_context(nc.sbuf_tensor("r_" + n, [128, w], F32))
                R["wtb"] = pp.enter_context(nc.sbuf_tensor("r_wtb", [128, 8], BF16))
                bb = bias_setup(pp, "p0", ["bqkv"])
            else:
                ROPE["cos"] = pp.enter_context(nc.sbuf_tensor("cosf", [128, S], BF16))
                ROPE["sin"] = pp.enter_context(nc.sbuf_tensor("sinf", [128, S], BF16))
                load(ROPE["cos"][:, :], cos_d[:, :])
                load(ROPE["sin"][:, :], sin_d[:, :])
                bb = bias_setup(pp, "p1", ["bq", "bk", "bv"])
            barrier()
            for th in range(2):
                with contextlib.ExitStack() as ph:
                    xnh = ph.enter_context(nc.sbuf_tensor(f"xnh{l}{th}", [128, NCH * 1024], BF16))
                    for c in range(NCH):
                        load(xnh[:, c * 1024:(c + 1) * 1024], xn_s[c, :, th * 1024:(th + 1) * 1024])
                    barrier()
                    if l == 0:
                        def sink_q(oc, tbg, pb):
                            sn0 = snap()
                            wait_snap("scalar", sn0, ("TE", "ST"))
                            sop("copy", out=qstg[:, :], in_=pb, s="SE")
                            wait("gpsimd", "SE")
                            store(q_s[oc, :, tbg * 512:(tbg + 1) * 512], qstg[:, :])
                        emit_proj(f"pq0{th}", xnh, th, wqkv_d, 0, list(range(8)),
                                  bb["bqkv"], 0, sink_q)

                        def sink_k(oc, tbg, pb):
                            wait("scalar", "TE")
                            sop("copy", out=kfm[:, oc * S + tbg * 512: oc * S + (tbg + 1) * 512],
                                in_=pb, s="SE")
                        emit_proj(f"pk0{th}", xnh, th, wqkv_d, 1024, list(range(8)),
                                  bb["bqkv"], 8, sink_k)

                        v3 = vaug[:, 0:NT * 1032].rearrange("p (tt c) -> p tt c", c=1032)

                        def sink_v(oc, tbg, pb):
                            sn0 = snap()
                            wait_snap("scalar", sn0, ("TE", "VE", "ST"))
                            sop("copy", out=qstg[:, :], in_=pb, s="SE")
                            wait("tensor", "SE")
                            wait("tensor", "VE")
                            for j in range(4):
                                mm(ps[:, 7 * 512 + j * 128:7 * 512 + (j + 1) * 128],
                                   qstg[:, j * 128:(j + 1) * 128], ident[:, :], True, True,
                                   s="TE" if j == 3 else None)
                            wait("vector", "TE")
                            vop("tensor_copy",
                                out=v3[:, tbg * 4:(tbg + 1) * 4, oc * 128:(oc + 1) * 128],
                                in_=bank(7).rearrange("p (j c) -> p j c", c=128), s="VE")
                        emit_proj(f"pv0{th}", xnh, th, wqkv_d, 2048, list(range(8)),
                                  bb["bqkv"], 16, sink_v, gsz=3)
                        emit_router_half(th, xnh, R)
                    else:
                        def sink_q1(oc, tbg, pb):
                            sn0 = snap()
                            wait_snap("scalar", sn0, ("TE", "VE"))
                            sop("copy", out=qstg[:, :], in_=pb, s="SE")
                            wait("vector", "SE")
                            wait("vector", "ST")
                            rope_emit(qrop[0:64, :], qrop[64:128, :], qstg[:, :],
                                      tbg * 512, (tbg + 1) * 512)
                            wait("gpsimd", "VE")
                            store(q_s[oc, :, tbg * 512:(tbg + 1) * 512], qrop[:, :])
                        emit_proj(f"pq1{th}", xnh, th, wq_d, 0, list(range(32)),
                                  bb["bq"], 0, sink_q1)

                        def sink_k1(oc, tbg, pb):
                            sn0 = snap()
                            wait_snap("scalar", sn0, ("TE", "VE"))
                            sop("copy", out=qstg[:, :], in_=pb, s="SE")
                            wait("vector", "SE")
                            dst = kfm[:, oc * S + tbg * 512: oc * S + (tbg + 1) * 512]
                            rope_emit(dst[0:64, :], dst[64:128, :], qstg[:, :],
                                      tbg * 512, (tbg + 1) * 512)
                        emit_proj(f"pk1{th}", xnh, th, wk_d, 0, list(range(8)),
                                  bb["bk"], 0, sink_k1)

                        v3 = vaug[:, 0:NT * 8 * 132].rearrange("p (tt g c) -> p tt g c", g=8, c=132)

                        def sink_v1(oc, tbg, pb):
                            sn0 = snap()
                            wait_snap("scalar", sn0, ("TE", "VE", "ST"))
                            sop("copy", out=qstg[:, :], in_=pb, s="SE")
                            wait("tensor", "SE")
                            wait("tensor", "VE")
                            for j in range(4):
                                mm(ps[:, 7 * 512 + j * 128:7 * 512 + (j + 1) * 128],
                                   qstg[:, j * 128:(j + 1) * 128], ident[:, :], True, True,
                                   s="TE" if j == 3 else None)
                            wait("vector", "TE")
                            vop("tensor_copy",
                                out=v3[:, tbg * 4:(tbg + 1) * 4, oc, 0:128],
                                in_=bank(7).rearrange("p (j c) -> p j c", c=128), s="VE")
                        emit_proj(f"pv1{th}", xnh, th, wv_d, 0, list(range(8)),
                                  bb["bv"], 0, sink_v1, gsz=3)
                barrier()

    def emit_attn0(kfm, vaug, resid):
        with contextlib.ExitStack() as ph:
            attn = ph.enter_context(nc.sbuf_tensor("attn0", [128, 8 * S], BF16))
            with contextlib.ExitStack() as pa:
                qsb = pa.enter_context(nc.sbuf_tensor("qsb0", [128, 8 * S], BF16))
                pt = pa.enter_context(nc.sbuf_tensor("pt0", [128, NT * 512], BF16))
                rz = pa.enter_context(nc.sbuf_tensor("rz0", [128, 512], F32))
                for c in range(8):
                    load(qsb[:, c * S:(c + 1) * S], q_s[c, :, :])
                barrier()
                v3 = vaug[:, 0:NT * 1032].rearrange("p (tt c) -> p tt c", c=1032)
                for qb in range(NTB):
                    for ktc in range(NT):
                        bi = ktc % 2
                        wait("tensor", "SE")
                        for dc in range(8):
                            mm(bank(bi), kfm[:, dc * S + ktc * 128: dc * S + (ktc + 1) * 128],
                               qsb[:, dc * S + qb * 512: dc * S + (qb + 1) * 512],
                               dc == 0, dc == 7, s="TE" if dc == 7 else None)
                        wait("scalar", "TE")
                        sop("activation", out=pt[:, ktc * 512:(ktc + 1) * 512], in_=bank(bi),
                            func=AF.Exp, scale=SC0, bias=-B0, s="SE")
                    wait("tensor", "SE")
                    wait("tensor", "VE")
                    for ktc in range(NT):
                        mm(bank(4), ones[:, 0:128], pt[:, ktc * 512:(ktc + 1) * 512],
                           ktc == 0, ktc == NT - 1)
                    mm(ps[0:1, 0:1], ones[0:1, 0:1], ones[0:1, 0:1], True, True, s="TE")
                    wait("vector", "TE")
                    vch("reciprocal", out=rz[:, :], in_=bank(4))
                    for dvc in range(8):
                        bi = 5 + dvc % 2
                        wait("tensor", "VE")
                        for ktc in range(NT):
                            mm(bank(bi), v3[:, ktc, dvc * 128:(dvc + 1) * 128],
                               pt[:, ktc * 512:(ktc + 1) * 512],
                               ktc == 0, ktc == NT - 1)
                        mm(ps[0:1, 2 * 512:2 * 512 + 1], ones[0:1, 0:1], ones[0:1, 0:1],
                           True, True, s="TE")
                        wait("vector", "TE")
                        vop("tensor_tensor",
                            out=attn[:, dvc * S + qb * 512: dvc * S + (qb + 1) * 512],
                            in0=bank(bi), in1=rz[:, :], op=ALU.mult, s="VE")
                barrier()
            with contextlib.ExitStack() as p2:
                wof = p2.enter_context(nc.sbuf_tensor("wof", [128, 512], F32))
                wob = p2.enter_context(nc.sbuf_tensor("wob", [128, 64 * 512], BF16))
                xt = p2.enter_context(nc.sbuf_tensor("xt0", [128, 512], F32))
                axs = p2.enter_context(nc.sbuf_tensor("axs0", [128, 512], F32))
                bb = bias_setup(p2, "wo", ["bo"])
                for dvc in range(8):
                    for ob in range(8):
                        wait("sync", "VE")
                        load(wof[:, :], wo_d[dvc * 128:(dvc + 1) * 128, ob * 512:(ob + 1) * 512])
                        wait("vector", "LD")
                        vop("tensor_copy",
                            out=wob[:, (dvc * 8 + ob) * 512:(dvc * 8 + ob + 1) * 512],
                            in_=wof[:, :], s="VE")
                barrier()
                for tt in range(NT):
                    for ob in range(8):
                        bi = ob % 2
                        wait("tensor", "VE")
                        for dvc in range(8):
                            mm(bank(bi), attn[:, dvc * S + tt * 128: dvc * S + (tt + 1) * 128],
                               wob[:, (dvc * 8 + ob) * 512:(dvc * 8 + ob + 1) * 512],
                               dvc == 0, False)
                        mm(bank(bi), ones[0:1, 0:128], bb["bo"][0:1, ob * 512:(ob + 1) * 512],
                           False, True)
                        mm(ps[0:1, 2 * 512:2 * 512 + 1], ones[0:1, 0:1], ones[0:1, 0:1],
                           True, True, s="TE")
                        wait("sync", "VE")
                        load(xt[:, :], resid[tt * 128:(tt + 1) * 128, ob * 512:(ob + 1) * 512])
                        wait("vector", "TE")
                        wait("vector", "LD")
                        wait("vector", "ST")
                        vop("tensor_tensor", out=axs[:, :], in0=bank(bi), in1=xt[:, :],
                            op=ALU.add, s="VE")
                        wait("gpsimd", "VE")
                        store(ax_s[tt * 128:(tt + 1) * 128, ob * 512:(ob + 1) * 512], axs[:, :])

    def emit_attn1(kfm, vaug, resid):
        with contextlib.ExitStack() as ph:
            qh = ph.enter_context(nc.sbuf_tensor("qh1", [128, 512], BF16))
            pt = ph.enter_context(nc.sbuf_tensor("pt1", [128, NT * 512], BF16))
            a1 = ph.enter_context(nc.sbuf_tensor("a1b", [128, 4 * D], BF16))
            rz4 = ph.enter_context(nc.sbuf_tensor("rz4", [128, 4], F32))
            x1t = ph.enter_context(nc.sbuf_tensor("x1t", [128, D], F32))
            axs = ph.enter_context(nc.sbuf_tensor("axs1", [128, D], F32))
            v3 = vaug[:, 0:NT * 8 * 132].rearrange("p (tt g c) -> p tt g c", g=8, c=132)
            pvb = ps[:, 2048:4096].rearrange("p (b c) -> p b c", c=512)
            for qb in range(NTB):
                for h in range(32):
                    g = h // 4
                    wait("sync", "TE")
                    load(qh[:, :], q_s[h, :, qb * 512:(qb + 1) * 512])
                    wait("tensor", "LD")
                    for kg in range(4):
                        wait("tensor", "SE")
                        for k4 in range(4):
                            ktc = kg * 4 + k4
                            mm(bank(k4), kfm[:, g * S + ktc * 128: g * S + (ktc + 1) * 128],
                               qh[:, :], True, True, s="TE" if k4 == 3 else None)
                        wait("scalar", "TE")
                        sop("activation", out=pt[:, kg * 2048:(kg + 1) * 2048],
                            in_=ps[:, 0:2048], func=AF.Exp, scale=SC1, bias=-B1, s="SE")
                    wait("tensor", "SE")
                    wait("tensor", "VE")
                    for qs in range(4):
                        for ktc in range(NT):
                            mm(ps[:, (4 + qs) * 512:(4 + qs) * 512 + 129],
                               pt[:, ktc * 512 + qs * 128: ktc * 512 + (qs + 1) * 128],
                               v3[:, ktc, g, 0:129],
                               ktc == 0, ktc == NT - 1)
                    mm(ps[0:1, 0:1], ones[0:1, 0:1], ones[0:1, 0:1], True, True, s="TE")
                    wait("vector", "TE")
                    vch("reciprocal", out=rz4[:, :], in_=pvb[:, :, 128:129].squeeze(2))
                    vop("tensor_tensor",
                        out=a1[:, :].rearrange("p (qs o) -> p qs o", o=D)[:, :, h * 128:(h + 1) * 128],
                        in0=pvb[:, :, 0:128],
                        in1=rz4[:, :, None].broadcast_to([128, 4, 128]),
                        op=ALU.mult, s="VE")
                wait("vector", "VE")
                for qs in range(4):
                    tt = qb * 4 + qs
                    wait("sync", "VE")
                    load(x1t[:, :], resid[tt * 128:(tt + 1) * 128, :], s="LX")
                    wait("vector", "LX")
                    wait("vector", "ST")
                    vop("tensor_tensor", out=axs[:, :], in0=a1[:, qs * D:(qs + 1) * D],
                        in1=x1t[:, :], op=ALU.add, s="VE")
                    wait("gpsimd", "VE")
                    store(ax_s[tt * 128:(tt + 1) * 128, :], axs[:, :])

    def emit_ffn(l):
        with contextlib.ExitStack() as ph:
            xnf = ph.enter_context(nc.sbuf_tensor(f"xnf{l}", [128, NCH * S], BF16))
            wt = ph.enter_context(nc.sbuf_tensor(f"wt{l}", [128, NCH * 512], BF16))
            fst = ph.enter_context(nc.sbuf_tensor(f"fst{l}", [128, 512], BF16))
            w4 = wt[:, :].rearrange("p (i c) -> p i c", c=512)
            if l == 0:
                bpp = ph.enter_context(nc.sbuf_tensor("bpp", [128, 4096], BF16))
                wbf = ph.enter_context(nc.sbuf_tensor("wbf", [128, 512], F32))
                wbb = ph.enter_context(nc.sbuf_tensor("wbb", [128, 512], BF16))
                bst = ph.enter_context(nc.sbuf_tensor("bst", [128, 128], F32))
                bstb = ph.enter_context(nc.sbuf_tensor("bstb", [128, 128], BF16))
                bb = bias_setup(ph, "ff", ["bbase"])
                for oc in range(NCH):
                    wait("sync", "VE")
                    for e in range(8):
                        load(bst[:, e * 16:(e + 1) * 16], bmat_d[e, oc * 128:(oc + 1) * 128, :])
                    wait("vector", "LD")
                    wait("vector", "TE")
                    vop("tensor_copy", out=bstb[:, :], in_=bst[:, :], s="VE")
                    wait("tensor", "VE")
                    mm(ps[:, 6 * 512:6 * 512 + 128], bstb[:, :], ident[:, :], True, True, s="TE")
                    wait("vector", "TE")
                    vop("tensor_copy", out=bpp[:, oc * 128:(oc + 1) * 128],
                        in_=ps[:, 6 * 512:6 * 512 + 128], s="VE")
                wait("tensor", "VE")
            else:
                pkb = ph.enter_context(nc.sbuf_tensor("pkb", [128, 2048], I32))
                scb = ph.enter_context(nc.sbuf_tensor("scb", [128, 128], F16))
                lo32 = ph.enter_context(nc.sbuf_tensor("lo32f", [128, 2048], I32))
                wnb = ph.enter_context(nc.sbuf_tensor("wnb", [128, 4096], BF16))
            for c in range(NCH):
                load(xnf[:, c * S:(c + 1) * S], xn_s[c, :, :])
            barrier()
            for ocg in range(8):
                for wr in range(4):
                    oc = ocg * 4 + wr
                    if l == 0:
                        for ib in range(8):
                            wait("sync", "VE")
                            load(wbf[:, :], wbase_d[oc * 128:(oc + 1) * 128, ib * 512:(ib + 1) * 512])
                            wait("vector", "LD")
                            wait("vector", "TE")
                            vop("tensor_copy", out=wbb[:, :], in_=wbf[:, :], s="VE")
                            wait("tensor", "VE")
                            for j in range(4):
                                mm(ps[:, 6 * 512 + j * 128:6 * 512 + (j + 1) * 128],
                                   wbb[:, j * 128:(j + 1) * 128], ident[:, :], True, True,
                                   s="TE" if j == 3 else None)
                            wait("vector", "TE")
                            vop("tensor_copy",
                                out=w4[:, ib * 4:(ib + 1) * 4, wr * 128:(wr + 1) * 128],
                                in_=bank(6).rearrange("p (j c) -> p j c", c=128), s="VE")
                    else:
                        wait("sync", "VE")
                        load(pkb[:, :], pk_d[oc * 128:(oc + 1) * 128, :])
                        load(scb[:, :], sc_d[oc * 128:(oc + 1) * 128, :])
                        wait("vector", "LD")
                        wait("vector", "TE")
                        w2 = wnb[:, :].rearrange("p (b r two) -> p b r two", b=128, r=16, two=2)
                        s_bc = scb[:, :, None].broadcast_to([128, 128, 16])
                        lo3 = lo32[:, :].rearrange("p (b r) -> p b r", r=16)
                        vch("tensor_single_scalar", out=lo32[:, :], in_=pkb[:, :], scalar=15,
                            op=ALU.bitwise_and)
                        vch("scalar_tensor_tensor", out=w2[:, :, :, 0], in0=lo3, scalar=-8.0,
                            in1=s_bc, op0=ALU.add, op1=ALU.mult)
                        vch("tensor_single_scalar", out=lo32[:, :], in_=pkb[:, :], scalar=4,
                            op=ALU.logical_shift_right)
                        vop("scalar_tensor_tensor", out=w2[:, :, :, 1], in0=lo3, scalar=-8.0,
                            in1=s_bc, op0=ALU.add, op1=ALU.mult, s="VE")
                        wait("tensor", "VE")
                        for ib in range(8):
                            wait("tensor", "VE")
                            for j in range(4):
                                mm(ps[:, 6 * 512 + j * 128:6 * 512 + (j + 1) * 128],
                                   wnb[:, (ib * 4 + j) * 128:(ib * 4 + j + 1) * 128],
                                   ident[:, :], True, True, s="TE" if j == 3 else None)
                            wait("vector", "TE")
                            vop("tensor_copy",
                                out=w4[:, ib * 4:(ib + 1) * 4, wr * 128:(wr + 1) * 128],
                                in_=bank(6).rearrange("p (j c) -> p j c", c=128), s="VE")
                wait("tensor", "VE")
                for tb in range(NTB):
                    sn = snap()
                    wait_snap("tensor", sn, ("SE",))
                    for inc_i in range(NCH):
                        last = (l == 1 and inc_i == NCH - 1)
                        for j in range(4):
                            mm(bank(j), wt[:, inc_i * 512 + j * 128: inc_i * 512 + (j + 1) * 128],
                               xnf[:, inc_i * S + tb * 512: inc_i * S + (tb + 1) * 512],
                               inc_i == 0, last,
                               s="TE" if (last and j == 3) else None)
                    if l == 0:
                        for j in range(4):
                            oc = ocg * 4 + j
                            mm(bank(j), bpp[:, oc * 128:(oc + 1) * 128],
                               hw[:, tb * 512:(tb + 1) * 512], False, False)
                            mm(bank(j), bb["bbase"][0:1, oc * 128:(oc + 1) * 128],
                               ones[0:1, 0:512], False, True,
                               s="TE" if j == 3 else None)
                    for j in range(4):
                        oc = ocg * 4 + j
                        wait("scalar", "TE")
                        wait("scalar", "ST")
                        sop("copy", out=fst[:, :], in_=bank(j), s="SE")
                        wait("gpsimd", "SE")
                        store(f_s[oc, :, tb * 512:(tb + 1) * 512], fst[:, :])

    def emit_fin(l, out_resid):
        with contextlib.ExitStack() as ph:
            frow = ph.enter_context(nc.sbuf_tensor(f"frow{l}", [128, 4 * S], BF16))
            axt = ph.enter_context(nc.sbuf_tensor(f"axt{l}", [128, 512], F32))
            ot = ph.enter_context(nc.sbuf_tensor(f"ot{l}", [128, 512], F32))
            for ocg in range(8):
                sn = snap()
                wait_snap("sync", sn, ("TE",))
                for j in (3, 2, 1, 0):
                    load(frow[:, j * S:(j + 1) * S], f_s[ocg * 4 + j, :, :], s="F2")
                wait("tensor", "F2")
                for tt in range(NT):
                    bi = tt % 2
                    wait("tensor", "VE")
                    for j in range(4):
                        mm(ps[:, bi * 512 + j * 128: bi * 512 + (j + 1) * 128],
                           frow[:, j * S + tt * 128: j * S + (tt + 1) * 128],
                           ident[:, :], True, True)
                    mm(ps[0:1, 2 * 512:2 * 512 + 1], ones[0:1, 0:1], ones[0:1, 0:1],
                       True, True, s="TE")
                    wait("sync", "VE")
                    load(axt[:, :], ax_s[tt * 128:(tt + 1) * 128, ocg * 512:(ocg + 1) * 512])
                    wait("vector", "TE")
                    wait("vector", "LD")
                    wait("vector", "ST")
                    vop("tensor_tensor", out=ot[:, :], in0=bank(bi), in1=axt[:, :],
                        op=ALU.add, s="VE")
                    wait("gpsimd", "VE")
                    store(out_resid[tt * 128:(tt + 1) * 128, ocg * 512:(ocg + 1) * 512], ot[:, :])

    layer_plan = ((x_d, r1_s), (r1_s, y_d))
    if dbg == 2:
        layer_plan = ((x_d, r1_s),)
    for l, (resid, outr) in enumerate(layer_plan):
        ph_ln(l, resid)
        with contextlib.ExitStack() as lay:
            kfm = lay.enter_context(nc.sbuf_tensor(f"kfm{l}", [128, 8 * S], BF16))
            vaug = lay.enter_context(
                nc.sbuf_tensor(f"vaug{l}", [128, NT * 8 * 132 if l == 1 else NT * 1032], BF16))
            if l == 1:
                vop("memset", vaug[:, :], 1.0, s="VE")
                wait("tensor", "VE")
                barrier()
            ph_proj(l, kfm, vaug)
            barrier()
            if dbg and l == 1:
                store(kfm_o, kfm[:, :])
                store(vaug_o, vaug[:, :])
                barrier()
            if l == 0:
                emit_attn0(kfm, vaug, resid)
            else:
                emit_attn1(kfm, vaug, resid)
            barrier()
        emit_ffn(l)
        barrier()
        emit_fin(l, outr)
        barrier()

    with es:
        with nc.Block() as block:
            @block.sync
            def _(eng):
                for f in prog["sync"]:
                    f(eng)

            @block.gpsimd
            def _(eng):
                for f in prog["gpsimd"]:
                    f(eng)

            @block.tensor
            def _(eng):
                for f in prog["tensor"]:
                    f(eng)

            @block.vector
            def _(eng):
                for f in prog["vector"]:
                    f(eng)

            @block.scalar
            def _(eng):
                for f in prog["scalar"]:
                    f(eng)
    return nc


def kernel(L0_attn_x, L0_attn_Wqkv, L0_attn_bqkv, L0_attn_Wo, L0_attn_bo,
           L0_ffn_W_base, L0_ffn_b_base, L0_ffn_W_router, L0_ffn_A, L0_ffn_B,
           L1_attn_wq, L1_attn_bq, L1_attn_wk, L1_attn_bk, L1_attn_wv, L1_attn_bv,
           L1_ffn_weight_packed, L1_ffn_scales):
    x = np.ascontiguousarray(np.asarray(L0_attn_x, dtype=np.float32))
    ident = np.eye(128, dtype=np.float32).astype(ml_dtypes.bfloat16)
    onesc = np.ones((128, 512), dtype=np.float32).astype(ml_dtypes.bfloat16)
    inv = 10000.0 ** (-np.arange(0, 128, 2, dtype=np.float32) / 128.0)
    ang = np.arange(S, dtype=np.float32)[:, None] * inv[None, :].astype(np.float32)
    c1t = np.cos(ang).T.astype(ml_dtypes.bfloat16)
    s1t = np.sin(ang).T.astype(ml_dtypes.bfloat16)
    cosc = np.ascontiguousarray(np.concatenate([c1t, c1t], axis=0))
    sinc = np.ascontiguousarray(np.concatenate([s1t, s1t], axis=0))
    repc = np.zeros((8, 128), dtype=np.float32)
    for e in range(8):
        repc[e, e * 16:(e + 1) * 16] = 1.0
    repc = repc.astype(ml_dtypes.bfloat16)

    sc = np.asarray(L1_ffn_scales)
    if sc.dtype != np.float16:
        sc = sc.view(np.float16) if sc.dtype.itemsize == 2 else sc.astype(np.float16)

    common = dict(
        wqkv=np.ascontiguousarray(np.asarray(L0_attn_Wqkv, np.float32)),
        bqkv=np.ascontiguousarray(np.asarray(L0_attn_bqkv, np.float32)),
        wo=np.ascontiguousarray(np.asarray(L0_attn_Wo, np.float32)),
        bo=np.ascontiguousarray(np.asarray(L0_attn_bo, np.float32)),
        wbase=np.ascontiguousarray(np.asarray(L0_ffn_W_base, np.float32)),
        bbase=np.ascontiguousarray(np.asarray(L0_ffn_b_base, np.float32)),
        wrout=np.ascontiguousarray(np.asarray(L0_ffn_W_router, np.float32)),
        amat=np.ascontiguousarray(np.asarray(L0_ffn_A, np.float32).reshape(128, D)),
        bmat=np.ascontiguousarray(np.asarray(L0_ffn_B, np.float32)),
        wq=np.ascontiguousarray(np.asarray(L1_attn_wq, np.float32)),
        bq=np.ascontiguousarray(np.asarray(L1_attn_bq, np.float32)),
        wk=np.ascontiguousarray(np.asarray(L1_attn_wk, np.float32)),
        bk=np.ascontiguousarray(np.asarray(L1_attn_bk, np.float32)),
        wv=np.ascontiguousarray(np.asarray(L1_attn_wv, np.float32)),
        bv=np.ascontiguousarray(np.asarray(L1_attn_bv, np.float32)),
        pk=np.ascontiguousarray(np.asarray(L1_ffn_weight_packed, np.int32)),
        sc=np.ascontiguousarray(sc),
        identc=ident, onesc=onesc, cosc=cosc, sinc=sinc, repc=repc,
    )
    nc = build()
    in_maps = [dict(common, x=np.ascontiguousarray(x[c])) for c in range(8)]
    res = run_bass_kernel_spmd(nc, in_maps, core_ids=list(range(8)))
    return np.stack([res.results[c]["y"] for c in range(8)], axis=0)
